# version 05
# baseline (speedup 1.0000x reference)
import numpy as np
import concourse.bass as bass
import concourse.mybir as mybir
from concourse.bass_utils import run_bass_kernel_spmd

F32 = mybir.dt.float32
BF16 = mybir.dt.bfloat16
AX = mybir.AxisListType.X
OP = mybir.AluOpType
ACT = mybir.ActivationFunctionType

B, S, D = 2, 4096, 4096
H, HD = 32, 128
NCORE, LANES = 8, 4
TLOC = 1024
NT = 8
NDM = 32
CLIP = 8.0
LN_EPS = 1e-5
NEG_BIG = -1.0e30
QSCALE = HD ** -0.5
SLOPES = [2.0 ** (-(h + 1) * 0.25) for h in range(H)]
NPAIR = 4

ENGINES = ("tensor", "vector", "scalar", "sync", "gpsimd")

PH = 6
DUMP = False


class _StopBuild(Exception):
    pass


class Sem:
    def __init__(self, handle):
        self.h = handle
        self.v = 0

    def inc(self, n=1):
        self.v += n
        return self.v


class Plan:
    def __init__(self):
        self.streams = {e: [] for e in ENGINES}
        self.waits = []

    def emit(self, eng, fn):
        self.streams[eng].append(fn)

    def wait(self, eng, sem, val=None):
        v = sem.v if val is None else val
        if v > 0:
            self.waits.append((sem, v))
            self.streams[eng].append(lambda e, s=sem.h, v=v: e.wait_ge(s, v))


def build_program():
    nc = bass.Bass("TRN2", target_bir_lowering=False, debug=False)

    x_in = nc.dram_tensor("x_shard", [TLOC, D], F32, kind="ExternalInput").ap()
    wqkvT = nc.dram_tensor("wqkvT", [D, 3 * D], F32, kind="ExternalInput").ap()
    woutT = nc.dram_tensor("woutT", [D, D], F32, kind="ExternalInput").ap()
    wffnT = nc.dram_tensor("wffnT", [D, D], F32, kind="ExternalInput").ap()
    wffn_nat = nc.dram_tensor("wffn_nat", [D, D], F32, kind="ExternalInput").ap()
    qln_w = nc.dram_tensor("qln_w", [D], F32, kind="ExternalInput").ap()
    qln_b = nc.dram_tensor("qln_b", [D], F32, kind="ExternalInput").ap()
    kln_w = nc.dram_tensor("kln_w", [D], F32, kind="ExternalInput").ap()
    kln_b = nc.dram_tensor("kln_b", [D], F32, kind="ExternalInput").ap()
    fbias = nc.dram_tensor("fbias", [D], F32, kind="ExternalInput").ap()
    ident_in = nc.dram_tensor("ident", [128, 128], F32, kind="ExternalInput").ap()
    colidx_in = nc.dram_tensor("colidx", [128, D], F32, kind="ExternalInput").ap()
    maskd_in = nc.dram_tensor("maskd", [128, 512], F32, kind="ExternalInput").ap()

    out_d = nc.dram_tensor("out", [TLOC, D], F32, kind="ExternalOutput").ap()

    sk = dict(kind="ExternalOutput") if DUMP else {}
    q_scr = nc.dram_tensor("q_scr", [TLOC, D], F32, **sk).ap()
    k_scr = nc.dram_tensor("k_scr", [TLOC, D], F32, **sk).ap()
    qT_scr = nc.dram_tensor("qT_scr", [D, TLOC], BF16, **sk).ap()
    ffnx_scr = nc.dram_tensor("ffnx_scr", [TLOC, D], F32, **sk).ap()
    alpha_scr = nc.dram_tensor("alpha_scr", [D], F32, **sk).ap()
    if DUMP:
        r8_dump = nc.dram_tensor("r8_dump", [128, 32768], BF16, kind="ExternalOutput").ap()
        sm_dump = nc.dram_tensor("sm_dump", [128, 64], F32, kind="ExternalOutput").ap()
        r4_dump = nc.dram_tensor("r4_dump", [128, 20480], BF16, kind="ExternalOutput").ap()
        w8_dump = nc.dram_tensor("w8_dump", [128, 32768], BF16, kind="ExternalOutput").ap()
        agk_dump = nc.dram_tensor("agk_dump", [LANES * D, TLOC], BF16, kind="ExternalOutput").ap()
        agv_dump = nc.dram_tensor("agv_dump", [LANES * TLOC, D], BF16, kind="ExternalOutput").ap()
    agk_in = nc.dram_tensor("agk_in", [D, TLOC], BF16)
    agv_in = nc.dram_tensor("agv_in", [TLOC, D], BF16)
    agk_out = nc.dram_tensor("agk_out", [LANES * D, TLOC], BF16)
    agv_out = nc.dram_tensor("agv_out", [LANES * TLOC, D], BF16)

    from contextlib import ExitStack
    ctx = ExitStack()
    E = ctx.enter_context

    r8 = E(nc.sbuf_tensor("r8", [128, 32768], BF16))
    w8 = E(nc.sbuf_tensor("w8", [128, 32768], BF16))
    r4 = E(nc.sbuf_tensor("r4", [128, 20480], BF16))
    rkv = E(nc.sbuf_tensor("rkv", [128, 16384], BF16))
    ident_f = E(nc.sbuf_tensor("ident_f", [128, 128], F32))
    ident_b = E(nc.sbuf_tensor("ident_b", [128, 128], BF16))
    maskd = E(nc.sbuf_tensor("maskd_sb", [128, 512], F32))
    comb = E(nc.sbuf_tensor("comb", [128, 1024], F32))
    wb = E(nc.sbuf_tensor("wb", [128, 128], F32))
    alpha_sb = E(nc.sbuf_tensor("alpha_sb", [128, 32], F32))
    sm = E(nc.sbuf_tensor("sm", [128, 64], F32))

    w8f = w8.ap().bitcast(F32)
    r4f = r4.ap().bitcast(F32)
    rkvf = rkv.ap().bitcast(F32)
    r8a = r8.ap()
    w8a = w8.ap()
    r4a = r4.ap()
    rkva = rkv.ap()

    ps = [E(nc.psum_tensor(f"ps{i}", [128, 512], F32)) for i in range(8)]

    sems = {}
    for name in ("xld0", "xld1", "wq0", "wq1", "stq0", "stq1",
                 "nat0", "nat1", "fsl0", "fsl1", "fxl0", "fxl1", "fst0", "fst1",
                 "wol0", "wol1", "wxl0", "wxl1", "wst0", "wst1",
                 "kv0", "kv1", "alb0", "alb1",
                 "ldc", "xld", "p1v", "p1t", "p1e",
                 "wq", "mmq", "evq", "stq",
                 "rowld", "p3v", "p3t", "p3e", "p3w", "agkw", "agvw",
                 "ccv", "cck",
                 "natld", "alv", "alw", "abc",
                 "fsl", "fsg", "fmm", "fev", "fxl", "fst",
                 "kvld", "qld", "alb", "smm", "sstt", "sdiag", "sexp", "smx", "cmb",
                 "ptm", "pte", "pvm", "ote",
                 "wol", "wmm", "wev", "wxl", "wst",
                 "bar", "vv", "ss"):
        sems[name] = Sem(E(nc.semaphore(name)))
    sem = sems

    plan = Plan()
    P = plan.emit
    W = plan.wait

    def barrier():
        snap = [(s, s.v) for s in sems.values() if s.v > 0 and s is not sem["bar"]]
        tgt = sem["bar"].v + len(ENGINES)
        sem["bar"].v = tgt
        for e in ENGINES:
            for s, v in snap:
                plan.waits.append((s, v))
                plan.streams[e].append(lambda eng, s=s.h, v=v: eng.wait_ge(s, v))
            plan.streams[e].append(lambda eng, s=sem["bar"].h: eng.nop().then_inc(s, 1))
        for e in ENGINES:
            plan.waits.append((sem["bar"], tgt))
            plan.streams[e].append(lambda eng, s=sem["bar"].h, t=tgt: eng.wait_ge(s, t))

    def checkpoint(k, dumps=()):
        if PH != k:
            return
        barrier()
        dsem = sems["ldc"]
        for dst, srcap in dumps:
            P("sync", lambda e, dst=dst, srcap=srcap, s=dsem.h:
              e.dma_start(out=dst, in_=srcap).then_inc(s, 16))
            dsem.inc(16)
        W("sync", dsem, dsem.v)
        raise _StopBuild

    def vsync():
        v = sem["vv"].inc(1)
        plan.waits.append((sem["vv"], v))
        plan.streams["vector"].append(lambda e, s=sem["vv"].h, v=v: e.wait_ge(s, v))
        return v

    def _body():
        for dst, src in ((ident_f[:, :], ident_in), (maskd[:, :], maskd_in)):
            P("sync", lambda e, dst=dst, src=src, s=sem["ldc"].h: e.dma_start(out=dst, in_=src).then_inc(s, 16))
            sem["ldc"].inc(16)
        for i, src in enumerate((qln_w, qln_b, kln_w, kln_b)):
            ap = src.rearrange("(t p) -> p t", p=128)
            P("sync", lambda e, ap=ap, i=i, s=sem["ldc"].h:
              e.dma_start(out=wb[:, i * 32:(i + 1) * 32], in_=ap, allow_slow_non_contiguous=True).then_inc(s, 16))
            sem["ldc"].inc(16)
        ldc_done = sem["ldc"].v

        W("vector", sem["ldc"], ldc_done)
        P("vector", lambda e: e.tensor_copy(ident_b[:, :], ident_f[:, :]))
        P("vector", lambda e: e.tensor_scalar_mul(wb[:, 0:32], wb[:, 0:32], QSCALE))
        P("vector", lambda e, s=sem["vv"].h: e.tensor_scalar_mul(wb[:, 32:64], wb[:, 32:64], QSCALE).then_inc(s, 1))
        vsync()

        sq_done = []

        def ln_stats(rowap, sqout):
            P("vector", lambda e, rowap=rowap, s=sem["vv"].h: e.reduce_sum(sm[:, 0:1], rowap, AX).then_inc(s, 1))
            vsync()
            P("scalar", lambda e, rowap=rowap, sqout=sqout, s=sem["ss"].h:
              e.activation(out=sqout, in_=rowap, func=ACT.Square, accum_out=sm[:, 1:2]).then_inc(s, 1))
            ssv = sem["ss"].inc(1)
            sq_done.append(ssv)
            P("vector", lambda e, s=sem["vv"].h: e.tensor_scalar_mul(sm[:, 2:3], sm[:, 0:1], 1.0 / D).then_inc(s, 1))
            vsync()
            P("vector", lambda e, s=sem["vv"].h:
              e.tensor_scalar(sm[:, 3:4], sm[:, 2:3], sm[:, 2:3], None, OP.mult).then_inc(s, 1))
            vsync()
            W("vector", sem["ss"], ssv)
            P("vector", lambda e, s=sem["vv"].h:
              e.tensor_scalar(sm[:, 4:5], sm[:, 1:2], 1.0 / D, sm[:, 3:4], OP.mult, OP.subtract).then_inc(s, 1))
            vsync()
            P("vector", lambda e, s=sem["vv"].h: e.tensor_scalar_add(sm[:, 5:6], sm[:, 4:5], LN_EPS).then_inc(s, 1))
            vsync()
            P("vector", lambda e, s=sem["vv"].h: e.reciprocal(sm[:, 6:7], sm[:, 5:6]).then_inc(s, 1))
            vsync()
            W("scalar", sem["vv"], sem["vv"].v)
            P("scalar", lambda e, s=sem["ss"].h: e.sqrt(sm[:, 7:8], sm[:, 6:7]).then_inc(s, 1))
            ssv = sem["ss"].inc(1)
            W("vector", sem["ss"], ssv)

        xrow = [w8f[:, 0:4096], w8f[:, 4096:8192]]
        xhat = [r4a[:, 0:4096], r4a[:, 4096:8192]]
        sqd = w8f[:, 8192:12288]
        r8v = r8.ap().rearrange("p (c n) -> p c n", c=32, n=1024)

        for tc in range(NT):
            pb = tc % 2
            if tc >= 2:
                W("sync", sem["p1v"], tc - 1)
                W("sync", sem["ss"], sq_done[tc - 2])
            xsem = sem["xld0" if tc % 2 == 0 else "xld1"]
            P("sync", lambda e, tc=tc, pb=pb, s=xsem.h:
              e.dma_start(out=xrow[pb], in_=x_in[tc * 128:(tc + 1) * 128, :]).then_inc(s, 16))
            xv = xsem.inc(16)
            W("vector", xsem, xv)
            W("scalar", xsem, xv)
            ln_stats(xrow[pb], sqd)
            if tc >= 2:
                W("vector", sem["p1t"], (tc - 1) * 8)
            P("vector", lambda e, pb=pb, s=sem["p1v"].h:
              e.tensor_scalar(xhat[pb], xrow[pb], sm[:, 2:3], sm[:, 7:8], OP.subtract, OP.mult).then_inc(s, 1))
            p1vv = sem["p1v"].inc(1)
            W("tensor", sem["p1v"], p1vv)
            for grp in range(8):
                bank = grp % 2
                psb = ps[bank].ap().bitcast(BF16)
                if sem["p1t"].v >= 2:
                    W("tensor", sem["p1e"], sem["p1e"].v - 1)
                for q in range(4):
                    dmc = grp * 4 + q
                    if q == 3:
                        P("tensor", lambda e, pb=pb, dmc=dmc, psb=psb, q=q, s=sem["p1t"].h:
                          e.transpose(psb[:, q * 128:(q + 1) * 128],
                                      xhat[pb][:, dmc * 128:(dmc + 1) * 128], ident_b[:, :]).then_inc(s, 1))
                    else:
                        P("tensor", lambda e, pb=pb, dmc=dmc, psb=psb, q=q:
                          e.transpose(psb[:, q * 128:(q + 1) * 128],
                                      xhat[pb][:, dmc * 128:(dmc + 1) * 128], ident_b[:, :]))
                p1tv = sem["p1t"].inc(1)
                W("vector", sem["p1t"], p1tv)
                d0 = grp * 4
                P("vector", lambda e, psb=psb, d0=d0, tc=tc, s=sem["p1e"].h:
                  e.tensor_copy(r8v[:, d0:d0 + 4, tc * 128:tc * 128 + 128],
                                psb[:, 0:512].rearrange("p (c n) -> p c n", c=4, n=128)).then_inc(s, 1))
                sem["p1e"].inc(1)

        checkpoint(1, dumps=[(r8_dump, r8a)] if DUMP else [])

        barrier()

        g_order = list(range(16, 24)) + list(range(0, 16))
        wslab = [w8a[:, 0:16384], w8a[:, 16384:32768]]
        stage_f = [r4f[:, 0:4096], r4f[:, 4096:8192]]
        stage_b = [rkva[:, 0:4096], rkva[:, 4096:8192]]
        wqkvT_r = wqkvT.rearrange("(c p) n -> p c n", p=128)
        agv_r = agv_in.ap().rearrange("(t p) n -> p t n", p=128)
        qsc_r = q_scr.rearrange("(t p) n -> p t n", p=128)
        ksc_r = k_scr.rearrange("(t p) n -> p t n", p=128)

        stg_dma = []

        for gi, g in enumerate(g_order):
            pb = gi % 2
            if gi >= 2:
                W("gpsimd", sem["mmq"], (gi - 1) * 256)
            wqsem = sem["wq0" if gi % 2 == 0 else "wq1"]
            P("gpsimd", lambda e, g=g, pb=pb, s=wqsem.h:
              e.dma_start(out=wslab[pb].rearrange("p (c n) -> p c n", c=32, n=512),
                          in_=wqkvT_r[:, :, g * 512:(g + 1) * 512]).then_inc(s, 16))
            wqv = wqsem.inc(16)
            W("tensor", wqsem, wqv)
            for half in range(2):
                if gi >= 1:
                    W("tensor", sem["evq"], (gi - 1) * 8 + (half + 1) * 4)
                for dmc in range(NDM):
                    for tcq in range(4):
                        tc = half * 4 + tcq
                        P("tensor", lambda e, pb=pb, dmc=dmc, tc=tc, s=sem["mmq"].h:
                          e.matmul(ps[tc][:, :],
                                   r8a[:, dmc * 1024 + tc * 128: dmc * 1024 + tc * 128 + 128],
                                   wslab[pb][:, dmc * 512:(dmc + 1) * 512],
                                   start=(dmc == 0), stop=(dmc == NDM - 1)).then_inc(s, 1))
                        sem["mmq"].inc(1)
                if gi >= 2:
                    W("vector", *stg_dma[gi - 2])
                W("vector", sem["mmq"], sem["mmq"].v)
                for tcq in range(4):
                    tc = half * 4 + tcq
                    dst = (stage_b[pb] if g >= 16 else stage_f[pb])[:, tc * 512:(tc + 1) * 512]
                    P("vector", lambda e, tc=tc, dst=dst, s=sem["evq"].h:
                      e.tensor_scalar(dst, ps[tc][:, :], CLIP, -CLIP, OP.min, OP.max).then_inc(s, 1))
                    sem["evq"].inc(1)
            W("sync", sem["evq"], sem["evq"].v)
            if g >= 16:
                dd = agv_r[:, :, (g - 16) * 512:(g - 15) * 512]
                src = stage_b[pb].rearrange("p (t n) -> p t n", t=8, n=512)
                ssm = sem["stq0" if gi % 2 == 0 else "stq1"]
            else:
                dd = (qsc_r if g < 8 else ksc_r)[:, :, (g % 8) * 512:((g % 8) + 1) * 512]
                src = stage_f[pb].rearrange("p (t n) -> p t n", t=8, n=512)
                ssm = sem["stq0" if gi % 2 == 0 else "stq1"]
            P("sync", lambda e, dd=dd, src=src, s=ssm.h: e.dma_start(out=dd, in_=src).then_inc(s, 16))
            stg_dma.append((ssm, ssm.inc(16)))
            if gi == 7:
                agv_done = [(sem["stq0"], sem["stq0"].v), (sem["stq1"], sem["stq1"].v)]

        checkpoint(2)

        for s_, v_ in agv_done:
            W("gpsimd", s_, v_)
        P("gpsimd", lambda e, s=sem["ccv"].h:
          e.collective_compute("AllGather", OP.bypass,
                               replica_groups=[[0, 1, 2, 3], [4, 5, 6, 7]],
                               ins=[agv_in.ap()], outs=[agv_out.ap()]).then_inc(s, 1))
        sem["ccv"].inc(1)

        barrier()

        rowbuf = [w8f[:, 0:4096], w8f[:, 4096:8192]]
        hatbuf = [w8f[:, 8192:12288], w8f[:, 12288:16384]]
        tstage = [r4a[:, 0:4096], r4a[:, 4096:8192]]
        qT_r = qT_scr.rearrange("(c p) n -> p c n", p=128)
        agk_r = agk_in.ap().rearrange("(c p) n -> p c n", p=128)

        p3row_guard = []
        p3st_dma = {0: [], 1: []}

        for tc in range(NT):
            for qk in range(2):
                idx = tc * 2 + qk
                pb = idx % 2
                if idx >= 2:
                    W("sync", sem["p3v"], idx - 1)
                    W("sync", sem["ss"], sq_done[8 + idx - 2])
                src_scr = q_scr if qk == 0 else k_scr
                P("sync", lambda e, src_scr=src_scr, tc=tc, pb=pb, s=sem["rowld"].h:
                  e.dma_start(out=rowbuf[pb], in_=src_scr[tc * 128:(tc + 1) * 128, :]).then_inc(s, 16))
                rv = sem["rowld"].inc(16)
                W("vector", sem["rowld"], rv)
                W("scalar", sem["rowld"], rv)
                if idx >= 2:
                    W("scalar", sem["p3t"], (idx - 1) * 8)
                ln_stats(rowbuf[pb], hatbuf[pb])
                P("vector", lambda e, pb=pb, s=sem["p3v"].h:
                  e.tensor_scalar(hatbuf[pb], rowbuf[pb], sm[:, 2:3], sm[:, 7:8], OP.subtract, OP.mult).then_inc(s, 1))
                p3vv = sem["p3v"].inc(1)
                W("tensor", sem["p3v"], p3vv)
                wcol = qk * 64
                if tc >= 1:
                    W("vector", *p3st_dma[qk][tc - 1])
                for grp in range(8):
                    bank = grp % 2
                    if sem["p3t"].v >= 2:
                        W("tensor", sem["p3e"], sem["p3e"].v - 1)
                    for q in range(4):
                        dmc = grp * 4 + q
                        if q == 3:
                            P("tensor", lambda e, pb=pb, dmc=dmc, bank=bank, q=q, s=sem["p3t"].h:
                              e.transpose(ps[bank][:, q * 128:(q + 1) * 128],
                                          hatbuf[pb][:, dmc * 128:(dmc + 1) * 128], ident_f[:, :]).then_inc(s, 1))
                        else:
                            P("tensor", lambda e, pb=pb, dmc=dmc, bank=bank, q=q:
                              e.transpose(ps[bank][:, q * 128:(q + 1) * 128],
                                          hatbuf[pb][:, dmc * 128:(dmc + 1) * 128], ident_f[:, :]))
                    p3tv = sem["p3t"].inc(1)
                    W("vector", sem["p3t"], p3tv)
                    for q in range(4):
                        dmc = grp * 4 + q
                        if q == 3:
                            P("vector", lambda e, qk=qk, bank=bank, q=q, dmc=dmc, wcol=wcol, s=sem["p3e"].h:
                              e.tensor_scalar(tstage[qk][:, dmc * 128:(dmc + 1) * 128],
                                              ps[bank][:, q * 128:(q + 1) * 128],
                                              wb[:, wcol + dmc:wcol + dmc + 1],
                                              wb[:, wcol + 32 + dmc:wcol + 32 + dmc + 1],
                                              OP.mult, OP.add).then_inc(s, 1))
                        else:
                            P("vector", lambda e, qk=qk, bank=bank, q=q, dmc=dmc, wcol=wcol:
                              e.tensor_scalar(tstage[qk][:, dmc * 128:(dmc + 1) * 128],
                                              ps[bank][:, q * 128:(q + 1) * 128],
                                              wb[:, wcol + dmc:wcol + dmc + 1],
                                              wb[:, wcol + 32 + dmc:wcol + 32 + dmc + 1],
                                              OP.mult, OP.add))
                    sem["p3e"].inc(1)
                W("sync", sem["p3e"], sem["p3e"].v)
                if qk == 0:
                    dd, ssm = qT_r[:, :, tc * 128:(tc + 1) * 128], sem["p3w"]
                else:
                    dd, ssm = agk_r[:, :, tc * 128:(tc + 1) * 128], sem["agkw"]
                P("sync", lambda e, dd=dd, qk=qk, s=ssm.h:
                  e.dma_start(out=dd, in_=tstage[qk].rearrange("p (c n) -> p c n", c=32, n=128)).then_inc(s, 16))
                p3st_dma[qk].append((ssm, ssm.inc(16)))

        W("gpsimd", sem["agkw"], 8 * 16)
        W("gpsimd", sem["ccv"], 1)
        P("gpsimd", lambda e, s=sem["cck"].h:
          e.collective_compute("AllGather", OP.bypass,
                               replica_groups=[[0, 1, 2, 3], [4, 5, 6, 7]],
                               ins=[agk_in.ap()], outs=[agk_out.ap()]).then_inc(s, 1))
        sem["cck"].inc(1)

        if DUMP and PH == 3:
            W("sync", sem["cck"], 1)
            checkpoint(3, dumps=[(agk_dump, agk_out.ap()), (agv_dump, agv_out.ap())])

        barrier()

        natbuf = [w8f[:, 0:4096], w8f[:, 4096:8192]]
        for t in range(32):
            pb = t % 2
            if t >= 2:
                W("sync", sem["alv"], t - 1)
            nsem = sem["nat0" if t % 2 == 0 else "nat1"]
            P("sync", lambda e, t=t, pb=pb, s=nsem.h:
              e.dma_start(out=natbuf[pb], in_=wffn_nat[t * 128:(t + 1) * 128, :]).then_inc(s, 16))
            nl = nsem.inc(16)
            W("vector", nsem, nl)
            P("vector", lambda e, pb=pb, t=t, s=sem["alv"].h:
              e.reduce_sum(alpha_sb[:, t:t + 1], natbuf[pb], AX, apply_absolute_value=True).then_inc(s, 1))
            sem["alv"].inc(1)
        W("vector", sem["alv"], 32)
        P("vector", lambda e, s=sem["vv"].h:
          e.tensor_scalar_mul(alpha_sb[:, :], alpha_sb[:, :], 1.0 / D).then_inc(s, 1))
        vsync()
        P("vector", lambda e, s=sem["alw"].h: e.nop().then_inc(s, 1))
        sem["alw"].inc(1)
        W("tensor", sem["alw"], sem["alw"].v)
        P("tensor", lambda e, s=sem["alw"].h:
          e.transpose(ps[0][0:32, 0:128], alpha_sb[:, 0:32], ident_f[:, :]).then_inc(s, 1))
        sem["alw"].inc(1)
        W("vector", sem["alw"], sem["alw"].v)
        alpha_row = rkvf[0:32, 2048:2176]
        P("vector", lambda e, s=sem["alw"].h:
          e.tensor_copy(alpha_row, ps[0][0:32, 0:128]).then_inc(s, 1))
        sem["alw"].inc(1)
        W("sync", sem["alw"], sem["alw"].v)
        P("sync", lambda e, s=sem["alw"].h:
          e.dma_start(out=alpha_scr.rearrange("(t q) -> t q", q=128), in_=alpha_row).then_inc(s, 16))
        sem["alw"].inc(16)
        W("sync", sem["alw"], sem["alw"].v)
        alpha_bc = r4f[:, 0:4096]
        bias_bc = r4f[:, 4096:8192]
        P("sync", lambda e, s=sem["abc"].h:
          e.dma_start(out=alpha_bc, in_=alpha_scr.unsqueeze(0).broadcast_to([128, D])).then_inc(s, 16))
        sem["abc"].inc(16)
        P("sync", lambda e, s=sem["abc"].h:
          e.dma_start(out=bias_bc, in_=fbias.unsqueeze(0).broadcast_to([128, D])).then_inc(s, 16))
        sem["abc"].inc(16)

        if PH == 41:
            W("sync", sem["abc"], sem["abc"].v)
            checkpoint(41)

        wffnT_r = wffnT.rearrange("(c p) n -> p c n", p=128)
        slabh = [w8a[:, i * 8192:(i + 1) * 8192] for i in range(2)]
        signh = [w8a[:, 16384 + i * 8192: 16384 + (i + 1) * 8192] for i in range(2)]
        xsl = [rkvf[:, 0:512], rkvf[:, 512:1024]]
        fstage = [rkvf[:, 1024:1536], rkvf[:, 1536:2048]]
        x_r = x_in.rearrange("(t p) n -> p t n", p=128)
        ffnx_r = ffnx_scr.rearrange("(t p) n -> p t n", p=128)

        for g in range(8):
            for hf in range(2):
                n = g * 2 + hf
                pb = n % 2
                if n < 2:
                    W("gpsimd", sem["alv"], 32)
                if n == 0:
                    W("tensor", sem["alw"], 3)
                if n >= 2:
                    W("gpsimd", sem["fmm"], (n - 1) * 128)
                fsem = sem["fsl0" if n % 2 == 0 else "fsl1"]
                P("gpsimd", lambda e, g=g, hf=hf, pb=pb, s=fsem.h:
                  e.dma_start(out=slabh[pb].rearrange("p (c n) -> p c n", c=16, n=512),
                              in_=wffnT_r[:, hf * 16:(hf + 1) * 16, g * 512:(g + 1) * 512]).then_inc(s, 16))
                fv = fsem.inc(16)
                W("scalar", fsem, fv)
                if n >= 2:
                    W("scalar", sem["fmm"], (n - 1) * 128)
                P("scalar", lambda e, pb=pb, s=sem["fsg"].h:
                  e.sign(signh[pb][:, :], slabh[pb][:, :]).then_inc(s, 1))
                sg = sem["fsg"].inc(1)
                W("tensor", sem["fsg"], sg)
                if g >= 1 and hf == 0:
                    W("tensor", sem["fev"], g * 8)
                for dq in range(16):
                    dmc = hf * 16 + dq
                    for tc in range(NT):
                        P("tensor", lambda e, pb=pb, dmc=dmc, dq=dq, tc=tc, s=sem["fmm"].h:
                          e.matmul(ps[tc][:, :],
                                   r8a[:, dmc * 1024 + tc * 128: dmc * 1024 + tc * 128 + 128],
                                   signh[pb][:, dq * 512:(dq + 1) * 512],
                                   start=(dmc == 0), stop=(dmc == NDM - 1)).then_inc(s, 1))
                        sem["fmm"].inc(1)
            W("vector", sem["abc"], 32)
            for tc in range(NT):
                n2 = g * 8 + tc
                pbx = n2 % 2
                if n2 >= 2:
                    W("sync", sem["fev"], n2 - 1)
                fxsem = sem["fxl0" if n2 % 2 == 0 else "fxl1"]
                P("sync", lambda e, g=g, tc=tc, pbx=pbx, s=fxsem.h:
                  e.dma_start(out=xsl[pbx], in_=x_r[:, tc, g * 512:(g + 1) * 512]).then_inc(s, 16))
                xv = fxsem.inc(16)
                W("vector", sem["fmm"], sem["fmm"].v)
                W("vector", fxsem, xv)
                fssem = sem["fst0" if n2 % 2 == 0 else "fst1"]
                if n2 >= 2:
                    W("vector", fssem, fssem.v)
                P("vector", lambda e, g=g, tc=tc, pbx=pbx, s=sem["vv"].h:
                  e.scalar_tensor_tensor(fstage[pbx], ps[tc][:, :], 1.0,
                                         alpha_bc[:, g * 512:(g + 1) * 512], OP.mult, OP.mult).then_inc(s, 1))
                vsync()
                P("vector", lambda e, g=g, pbx=pbx, s=sem["vv"].h:
                  e.tensor_add(fstage[pbx], fstage[pbx], bias_bc[:, g * 512:(g + 1) * 512]).then_inc(s, 1))
                vsync()
                P("vector", lambda e, pbx=pbx, s=sem["fev"].h:
                  e.tensor_add(fstage[pbx], fstage[pbx], xsl[pbx]).then_inc(s, 1))
                sem["fev"].inc(1)
                W("sync", sem["fev"], sem["fev"].v)
                P("sync", lambda e, g=g, tc=tc, pbx=pbx, s=fssem.h:
                  e.dma_start(out=ffnx_r[:, tc, g * 512:(g + 1) * 512], in_=fstage[pbx]).then_inc(s, 16))
                fssem.inc(16)
            if g == 0 and PH == 42:
                W("sync", sem["fst0"], sem["fst0"].v)
                W("sync", sem["fst1"], sem["fst1"].v)
                checkpoint(42)

        checkpoint(4)

        barrier()
        W("tensor", sem["cck"], 1)

        OT = r8a
        Sb = [w8f[:, 0:4096], w8f[:, 4096:8192]]
        alibi_b = [w8f[:, 8192:12288], w8f[:, 12288:16384]]
        Pb = [[r4a[:, 0:4096], r4a[:, 4096:8192]],
              [r4a[:, 8192:12288], r4a[:, 12288:16384]]]
        PT = [r4a[:, 16384:16896], r4a[:, 16896:17408]]
        diag = [[r4a[:, 17408:17536], r4a[:, 17536:17664]],
                [r4a[:, 17664:17792], r4a[:, 17792:17920]]]
        Qh = [r4a[:, 17920:18944], r4a[:, 18944:19968]]
        Kh = [rkva[:, 0:4096], rkva[:, 4096:8192]]
        Vh = [rkva[:, 8192:12288], rkva[:, 12288:16384]]
        combf = comb.ap()
        combA = [combf[:, 0:512], combf[:, 0:512]]
        combB = [combf[:, 512:1024], combf[:, 512:1024]]

        agk_o = agk_out.ap()
        agv_o = agv_out.ap()

        def emit_kv_loads(h):
            pb = h % 2
            ksem = sem["kv0" if h % 2 == 0 else "kv1"]
            for r in range(LANES):
                kdst = Kh[pb].rearrange("p (i n) -> p i n", i=8, n=512)[:, :, r * 128:(r + 1) * 128]
                ksrc = agk_o[r * D + h * 128: r * D + (h + 1) * 128, :].rearrange("p (i n) -> p i n", i=8, n=128)
                P("sync", lambda e, kdst=kdst, ksrc=ksrc, s=ksem.h:
                  e.dma_start(out=kdst, in_=ksrc).then_inc(s, 16))
                ksem.inc(16)
                vdst = Vh[pb].rearrange("p (i n) -> p i n", i=8, n=512)[:, :, r * 128:(r + 1) * 128]
                vsrc = agv_o[r * TLOC:(r + 1) * TLOC, h * 128:(h + 1) * 128].rearrange("(i p) n -> p i n", p=128)
                P("sync", lambda e, vdst=vdst, vsrc=vsrc, s=ksem.h:
                  e.dma_start(out=vdst, in_=vsrc).then_inc(s, 16))
                ksem.inc(16)
            P("sync", lambda e, h=h, pb=pb, s=ksem.h:
              e.dma_start(out=Qh[pb], in_=qT_scr[h * 128:(h + 1) * 128, :]).then_inc(s, 16))
            ksem.inc(16)

        W("sync", sem["cck"], 1)
        emit_kv_loads(0)

        def emit_alibi(h):
            hb = h % 2
            asem = sem["alb0" if h % 2 == 0 else "alb1"]
            P("sync", lambda e, hb=hb, s=asem.h:
              e.dma_start(out=alibi_b[hb], in_=colidx_in).then_inc(s, 16))
            av = asem.inc(16)
            W("gpsimd", asem, av)
            P("gpsimd", lambda e, hb=hb, h=h, s=sem["alb"].h:
              e.tensor_scalar_mul(alibi_b[hb], alibi_b[hb], SLOPES[h]).then_inc(s, 1))
            sem["alb"].inc(1)

        emit_alibi(0)

        def emit_stage_SP(h, j):
            sidx = h * NPAIR + j
            pp = sidx % 2
            hb = h % 2
            nckA, nckB = 2 * j + 1, 2 * j + 2
            kmax = nckB * 512
            W("vector", sem["alb"], sem["alb"].v)
            dA0, dB0 = 2 * j * 512, (2 * j + 1) * 512
            W("gpsimd", sem["alb"], sem["alb"].v)
            W("gpsimd", sem["sstt"], sem["sstt"].v)
            P("gpsimd", lambda e, hb=hb, dA0=dA0, pp=pp:
              e.tensor_add(combA[pp], maskd[:, :], alibi_b[hb][:, dA0:dA0 + 512]))
            P("gpsimd", lambda e, hb=hb, dB0=dB0, pp=pp, s=sem["cmb"].h:
              e.tensor_add(combB[pp], maskd[:, :], alibi_b[hb][:, dB0:dB0 + 512]).then_inc(s, 1))
            sem["cmb"].inc(1)
            for t in range(2):
                nck = nckA if t == 0 else nckB
                qcol = (2 * j + t) * 128
                W("vector", sem["sexp"], max(0, 2 * (sidx - 1) + t + 1))
                W("vector", sem["cmb"], sem["cmb"].v)
                for kc in range(nck):
                    bank = sem["smm"].v % 2
                    if sem["smm"].v >= 2:
                        W("tensor", sem["sstt"], sem["smm"].v - 1)
                    P("tensor", lambda e, hb=hb, qcol=qcol, kc=kc, bank=bank, s=sem["smm"].h:
                      e.matmul(ps[bank][:, :], Qh[hb][:, qcol:qcol + 128],
                               Kh[hb][:, kc * 512:(kc + 1) * 512], start=True, stop=True).then_inc(s, 1))
                    mmv = sem["smm"].inc(1)
                    W("vector", sem["smm"], mmv)
                    isdiag = (kc == nck - 1)
                    in1 = (combA[pp] if t == 0 else combB[pp]) if isdiag else None
                    if isdiag:
                        P("vector", lambda e, t=t, bank=bank, kc=kc, in1=in1, s=sem["sstt"].h:
                          e.scalar_tensor_tensor(Sb[t][:, kc * 512:(kc + 1) * 512], ps[bank][:, :], 1.0,
                                                 in1, OP.mult, OP.add).then_inc(s, 1))
                    else:
                        P("vector", lambda e, t=t, bank=bank, kc=kc, hb=hb, s=sem["sstt"].h:
                          e.scalar_tensor_tensor(Sb[t][:, kc * 512:(kc + 1) * 512], ps[bank][:, :], 1.0,
                                                 alibi_b[hb][:, kc * 512:(kc + 1) * 512],
                                                 OP.mult, OP.add).then_inc(s, 1))
                    sem["sstt"].inc(1)
                W("vector", sem["sstt"], sem["sstt"].v)
                P("vector", lambda e, t=t, nck=nck, s=sem["smx"].h:
                  e.reduce_max(sm[:, 32 + t:33 + t], Sb[t][:, 0:nck * 512], AX, negate=True).then_inc(s, 1))
                smxv = sem["smx"].inc(1)
                W("scalar", sem["smx"], smxv)
                W("scalar", sem["ptm"], sem["ptm"].v)
                P("scalar", lambda e, t=t, pp=pp, nck=nck, s=sem["sexp"].h:
                  e.activation(out=Pb[pp][t][:, 0:nck * 512], in_=Sb[t][:, 0:nck * 512], func=ACT.Exp,
                               bias=sm[:, 32 + t:33 + t], scale=1.0,
                               accum_out=sm[:, 34 + t:35 + t]).then_inc(s, 1))
                exv = sem["sexp"].inc(1)
                if t == 0:
                    W("gpsimd", sem["ptm"], sem["ptm"].v)
                    P("gpsimd", lambda e, pp=pp, nckA=nckA, kmax=kmax:
                      e.memset(Pb[pp][0][:, nckA * 512:kmax], 0.0))
                W("vector", sem["sexp"], exv)
                P("vector", lambda e, t=t, s=sem["vv"].h:
                  e.reciprocal(sm[:, 36 + t:37 + t], sm[:, 34 + t:35 + t]).then_inc(s, 1))
                rv = vsync()
                W("gpsimd", sem["vv"], rv)
                P("gpsimd", lambda e, t=t, pp=pp, s=sem["sdiag"].h:
                  e.tensor_scalar_mul(diag[pp][t], ident_b[:, :], sm[:, 36 + t:37 + t]).then_inc(s, 1))
                sem["sdiag"].inc(1)
            return dict(h=h, j=j, pp=pp, hb=hb, sidx=sidx, sdiag=sem["sdiag"].v)

        def emit_stage_PT_PV(info):
            h, j, pp, hb, sidx = info["h"], info["j"], info["pp"], info["hb"], info["sidx"]
            nkc = (2 * j + 2) * 4
            otb = 4 + sidx % 2
            W("tensor", sem["sdiag"], info["sdiag"])
            if sidx >= 2:
                W("tensor", sem["ote"], sidx - 1)
            for kc2 in range(0, nkc, 2):
                ptb = 2 + (kc2 // 2) % 2
                ptp = (kc2 // 2) % 2
                if sem["pte"].v >= 2:
                    W("tensor", sem["pte"], sem["pte"].v - 1)
                for sub in range(2):
                    kc = kc2 + sub
                    for t in range(2):
                        if sub == 1 and t == 1:
                            P("tensor", lambda e, pp=pp, t=t, kc=kc, sub=sub, ptb=ptb, s=sem["ptm"].h:
                              e.matmul(ps[ptb][:, sub * 256 + t * 128: sub * 256 + (t + 1) * 128],
                                       Pb[pp][t][:, kc * 128:(kc + 1) * 128], diag[pp][t],
                                       start=True, stop=True).then_inc(s, 1))
                        else:
                            P("tensor", lambda e, pp=pp, t=t, kc=kc, sub=sub, ptb=ptb:
                              e.matmul(ps[ptb][:, sub * 256 + t * 128: sub * 256 + (t + 1) * 128],
                                       Pb[pp][t][:, kc * 128:(kc + 1) * 128], diag[pp][t],
                                       start=True, stop=True))
                ptv = sem["ptm"].inc(1)
                W("scalar", sem["ptm"], ptv)
                if sem["pte"].v >= 2:
                    W("scalar", sem["pvm"], sem["pvm"].v - 1)
                P("scalar", lambda e, ptb=ptb, ptp=ptp, s=sem["pte"].h:
                  e.copy(PT[ptp][:, :], ps[ptb][:, :]).then_inc(s, 1))
                ptev = sem["pte"].inc(1)
                W("tensor", sem["pte"], ptev)
                for sub in range(2):
                    kc = kc2 + sub
                    if sub == 1:
                        P("tensor", lambda e, hb=hb, kc=kc, sub=sub, ptp=ptp, otb=otb,
                          st=(kc == 0), sp=(kc == nkc - 1), s=sem["pvm"].h:
                          e.matmul(ps[otb][:, 0:256], Vh[hb][:, kc * 128:(kc + 1) * 128],
                                   PT[ptp][:, sub * 256:(sub + 1) * 256], start=st, stop=sp).then_inc(s, 1))
                    else:
                        P("tensor", lambda e, hb=hb, kc=kc, sub=sub, ptp=ptp, otb=otb,
                          st=(kc == 0), sp=(kc == nkc - 1):
                          e.matmul(ps[otb][:, 0:256], Vh[hb][:, kc * 128:(kc + 1) * 128],
                                   PT[ptp][:, sub * 256:(sub + 1) * 256], start=st, stop=sp))
                sem["pvm"].inc(1)
            W("vector", sem["pvm"], sem["pvm"].v)
            P("vector", lambda e, h=h, j=j, otb=otb, s=sem["ote"].h:
              e.tensor_copy(OT[:, h * 1024 + j * 256: h * 1024 + (j + 1) * 256],
                            ps[otb][:, 0:256]).then_inc(s, 1))
            sem["ote"].inc(1)

        H_eff = {51: 1, 52: 4, 53: 16}.get(PH, H)
        prev_info = None
        for h in range(H_eff):
            W("tensor", sem["kv0" if h % 2 == 0 else "kv1"], 144 * (h // 2 + 1))
            for j in range(NPAIR):
                info = emit_stage_SP(h, j)
                if prev_info is not None:
                    emit_stage_PT_PV(prev_info)
                prev_info = info
                if j == 0 and h + 1 < H_eff:
                    if h >= 1:
                        W("sync", sem["ote"], h * NPAIR)
                    emit_kv_loads(h + 1)
                    emit_alibi(h + 1)
        emit_stage_PT_PV(prev_info)

        if PH in (51, 52, 53):
            checkpoint(PH, dumps=[(r8_dump, r8a), (sm_dump, sm[:, :]),
                                  (r4_dump, r4a), (w8_dump, w8a)] if DUMP else [])
        checkpoint(5, dumps=[(r8_dump, r8a)] if DUMP else [])

        barrier()

        woutT_r = woutT.rearrange("(c p) n -> p c n", p=128)
        wslab2 = [w8a[:, 0:16384], w8a[:, 16384:32768]]
        fxsl = [rkvf[:, 0:512], rkvf[:, 512:1024]]
        ostage = [rkvf[:, 1024:1536], rkvf[:, 1536:2048]]
        out_r = out_d.rearrange("(t p) n -> p t n", p=128)

        for g in range(8):
            pb = g % 2
            if g >= 2:
                W("gpsimd", sem["wmm"], (g - 1) * 256)
            wlsem = sem["wol0" if g % 2 == 0 else "wol1"]
            P("gpsimd", lambda e, g=g, pb=pb, s=wlsem.h:
              e.dma_start(out=wslab2[pb].rearrange("p (c n) -> p c n", c=32, n=512),
                          in_=woutT_r[:, :, g * 512:(g + 1) * 512]).then_inc(s, 16))
            wv = wlsem.inc(16)
            W("tensor", wlsem, wv)
            if g >= 1:
                W("tensor", sem["wev"], g * 8)
            for dmc in range(NDM):
                for tc in range(NT):
                    P("tensor", lambda e, pb=pb, dmc=dmc, tc=tc, s=sem["wmm"].h:
                      e.matmul(ps[tc][:, :],
                               r8a[:, dmc * 1024 + tc * 128: dmc * 1024 + tc * 128 + 128],
                               wslab2[pb][:, dmc * 512:(dmc + 1) * 512],
                               start=(dmc == 0), stop=(dmc == NDM - 1)).then_inc(s, 1))
                    sem["wmm"].inc(1)
            for tc in range(NT):
                n2 = g * 8 + tc
                pbx = n2 % 2
                if n2 >= 2:
                    W("sync", sem["wev"], n2 - 1)
                wxsem = sem["wxl0" if n2 % 2 == 0 else "wxl1"]
                P("sync", lambda e, g=g, tc=tc, pbx=pbx, s=wxsem.h:
                  e.dma_start(out=fxsl[pbx], in_=ffnx_r[:, tc, g * 512:(g + 1) * 512]).then_inc(s, 16))
                xv = wxsem.inc(16)
                W("vector", sem["wmm"], sem["wmm"].v)
                W("vector", wxsem, xv)
                wssem = sem["wst0" if n2 % 2 == 0 else "wst1"]
                if n2 >= 2:
                    W("vector", wssem, wssem.v)
                P("vector", lambda e, tc=tc, pbx=pbx, s=sem["wev"].h:
                  e.tensor_add(ostage[pbx], ps[tc][:, :], fxsl[pbx]).then_inc(s, 1))
                sem["wev"].inc(1)
                W("sync", sem["wev"], sem["wev"].v)
                P("sync", lambda e, g=g, tc=tc, pbx=pbx, s=wssem.h:
                  e.dma_start(out=out_r[:, tc, g * 512:(g + 1) * 512], in_=ostage[pbx]).then_inc(s, 16))
                wssem.inc(16)

        W("sync", sem["wst0"], sem["wst0"].v)
        W("sync", sem["wst1"], sem["wst1"].v)


    try:
        _body()
    except _StopBuild:
        pass

    for s, v in plan.waits:
        assert v <= s.v, f"wait {v} > final {s.v} on sem"

    block = E(nc.Block())

    @block.tensor
    def _(t):
        for fn in plan.streams["tensor"]:
            fn(t)

    @block.vector
    def _(v):
        for fn in plan.streams["vector"]:
            fn(v)

    @block.scalar
    def _(s):
        for fn in plan.streams["scalar"]:
            fn(s)

    @block.sync
    def _(s):
        for fn in plan.streams["sync"]:
            fn(s)

    @block.gpsimd
    def _(g):
        for fn in plan.streams["gpsimd"]:
            fn(g)

    ctx.close()
    return nc


_NC_CACHE = None


def kernel(L0_attn_hidden_states, L0_attn_Wqkv, L0_attn_q_ln_w, L0_attn_q_ln_b,
           L0_attn_k_ln_w, L0_attn_k_ln_b, L0_attn_Wout, L0_ffn_weight_real,
           L0_ffn_bias):
    global _NC_CACHE
    x = np.ascontiguousarray(L0_attn_hidden_states, dtype=np.float32)
    wqkvT = np.ascontiguousarray(np.asarray(L0_attn_Wqkv, dtype=np.float32).T)
    woutT = np.ascontiguousarray(np.asarray(L0_attn_Wout, dtype=np.float32).T)
    wffn = np.ascontiguousarray(np.asarray(L0_ffn_weight_real, dtype=np.float32))
    wffnT = np.ascontiguousarray(wffn.T)
    qlw = np.ascontiguousarray(L0_attn_q_ln_w, dtype=np.float32)
    qlb = np.ascontiguousarray(L0_attn_q_ln_b, dtype=np.float32)
    klw = np.ascontiguousarray(L0_attn_k_ln_w, dtype=np.float32)
    klb = np.ascontiguousarray(L0_attn_k_ln_b, dtype=np.float32)
    fb = np.ascontiguousarray(L0_ffn_bias, dtype=np.float32)
    ident = np.eye(128, dtype=np.float32)
    colidx = np.broadcast_to(np.arange(D, dtype=np.float32), (128, D)).copy()

    in_maps = []
    for c in range(NCORE):
        b, lane = c // 4, c % 4
        xs = np.concatenate([x[b, (lane + 4 * i) * 128:(lane + 4 * i) * 128 + 128, :]
                             for i in range(NT)], axis=0)
        md = np.where(np.arange(512)[None, :] <= lane * 128 + np.arange(128)[:, None],
                      0.0, NEG_BIG).astype(np.float32)
        in_maps.append({
            "x_shard": np.ascontiguousarray(xs),
            "wqkvT": wqkvT, "woutT": woutT, "wffnT": wffnT, "wffn_nat": wffn,
            "qln_w": qlw, "qln_b": qlb, "kln_w": klw, "kln_b": klb, "fbias": fb,
            "ident": ident, "colidx": colidx, "maskd": md,
        })

    if _NC_CACHE is None:
        _NC_CACHE = build_program()
    res = run_bass_kernel_spmd(_NC_CACHE, in_maps, core_ids=list(range(NCORE)))

    out = np.empty((B, S, D), dtype=np.float32)
    for c in range(NCORE):
        b, lane = c // 4, c % 4
        sh = res.results[c]["out"]
        for i in range(NT):
            out[b, (lane + 4 * i) * 128:(lane + 4 * i) * 128 + 128, :] = sh[i * 128:(i + 1) * 128, :]
    return out


# version 06 (final)
# speedup vs baseline: 1.1493x; 1.1493x over previous
import numpy as np
import concourse.bass as bass
import concourse.mybir as mybir
from concourse.bass_utils import run_bass_kernel_spmd

F32 = mybir.dt.float32
BF16 = mybir.dt.bfloat16
AX = mybir.AxisListType.X
OP = mybir.AluOpType
ACT = mybir.ActivationFunctionType

B, S, D = 2, 4096, 4096
H, HD = 32, 128
NCORE, LANES = 8, 4
TLOC = 1024
NT = 8
NDM = 32
CLIP = 8.0
LN_EPS = 1e-5
NEG_BIG = -1.0e30
QSCALE = HD ** -0.5
SLOPES = [2.0 ** (-(h + 1) * 0.25) for h in range(H)]
NPAIR = 4

ENGINES = ("tensor", "vector", "scalar", "sync", "gpsimd")

PH = 6
DUMP = False


class _StopBuild(Exception):
    pass


class Sem:
    def __init__(self, handle):
        self.h = handle
        self.v = 0

    def inc(self, n=1):
        self.v += n
        return self.v


class Plan:
    def __init__(self):
        self.streams = {e: [] for e in ENGINES}
        self.waits = []

    def emit(self, eng, fn):
        self.streams[eng].append(fn)

    def wait(self, eng, sem, val=None):
        v = sem.v if val is None else val
        if v > 0:
            self.waits.append((sem, v))
            self.streams[eng].append(lambda e, s=sem.h, v=v: e.wait_ge(s, v))


def build_program():
    nc = bass.Bass("TRN2", target_bir_lowering=False, debug=False)

    x_in = nc.dram_tensor("x_shard", [TLOC, D], F32, kind="ExternalInput").ap()
    wqkvT = nc.dram_tensor("wqkvT", [D, 3 * D], F32, kind="ExternalInput").ap()
    woutT = nc.dram_tensor("woutT", [D, D], F32, kind="ExternalInput").ap()
    wffnT = nc.dram_tensor("wffnT", [D, D], F32, kind="ExternalInput").ap()
    wffn_nat = nc.dram_tensor("wffn_nat", [D, D], F32, kind="ExternalInput").ap()
    qln_w = nc.dram_tensor("qln_w", [D], F32, kind="ExternalInput").ap()
    qln_b = nc.dram_tensor("qln_b", [D], F32, kind="ExternalInput").ap()
    kln_w = nc.dram_tensor("kln_w", [D], F32, kind="ExternalInput").ap()
    kln_b = nc.dram_tensor("kln_b", [D], F32, kind="ExternalInput").ap()
    fbias = nc.dram_tensor("fbias", [D], F32, kind="ExternalInput").ap()
    ident_in = nc.dram_tensor("ident", [128, 128], F32, kind="ExternalInput").ap()
    colidx_in = nc.dram_tensor("colidx", [128, D], F32, kind="ExternalInput").ap()
    maskd_in = nc.dram_tensor("maskd", [128, 512], F32, kind="ExternalInput").ap()

    out_d = nc.dram_tensor("out", [TLOC, D], F32, kind="ExternalOutput").ap()

    sk = dict(kind="ExternalOutput") if DUMP else {}
    q_scr = nc.dram_tensor("q_scr", [TLOC, D], F32, **sk).ap()
    k_scr = nc.dram_tensor("k_scr", [TLOC, D], F32, **sk).ap()
    qT_scr = nc.dram_tensor("qT_scr", [D, TLOC], BF16, **sk).ap()
    ffnx_scr = nc.dram_tensor("ffnx_scr", [TLOC, D], F32, **sk).ap()
    alpha_scr = nc.dram_tensor("alpha_scr", [D], F32, **sk).ap()
    if DUMP:
        r8_dump = nc.dram_tensor("r8_dump", [128, 32768], BF16, kind="ExternalOutput").ap()
        sm_dump = nc.dram_tensor("sm_dump", [128, 64], F32, kind="ExternalOutput").ap()
        r4_dump = nc.dram_tensor("r4_dump", [128, 20480], BF16, kind="ExternalOutput").ap()
        w8_dump = nc.dram_tensor("w8_dump", [128, 32768], BF16, kind="ExternalOutput").ap()
        agk_dump = nc.dram_tensor("agk_dump", [LANES * D, TLOC], BF16, kind="ExternalOutput").ap()
        agv_dump = nc.dram_tensor("agv_dump", [LANES * TLOC, D], BF16, kind="ExternalOutput").ap()
    agk_in = nc.dram_tensor("agk_in", [D, TLOC], BF16)
    agv_in = nc.dram_tensor("agv_in", [TLOC, D], BF16)
    agk_out = nc.dram_tensor("agk_out", [LANES * D, TLOC], BF16)
    agv_out = nc.dram_tensor("agv_out", [LANES * TLOC, D], BF16)

    from contextlib import ExitStack
    ctx = ExitStack()
    E = ctx.enter_context

    r8 = E(nc.sbuf_tensor("r8", [128, 32768], BF16))
    w8 = E(nc.sbuf_tensor("w8", [128, 32768], BF16))
    r4 = E(nc.sbuf_tensor("r4", [128, 20480], BF16))
    rkv = E(nc.sbuf_tensor("rkv", [128, 16384], BF16))
    ident_f = E(nc.sbuf_tensor("ident_f", [128, 128], F32))
    ident_b = E(nc.sbuf_tensor("ident_b", [128, 128], BF16))
    maskd = E(nc.sbuf_tensor("maskd_sb", [128, 512], F32))
    comb = E(nc.sbuf_tensor("comb", [128, 1024], F32))
    wb = E(nc.sbuf_tensor("wb", [128, 128], F32))
    alpha_sb = E(nc.sbuf_tensor("alpha_sb", [128, 32], F32))
    sm = E(nc.sbuf_tensor("sm", [128, 64], F32))

    w8f = w8.ap().bitcast(F32)
    r4f = r4.ap().bitcast(F32)
    rkvf = rkv.ap().bitcast(F32)
    r8a = r8.ap()
    w8a = w8.ap()
    r4a = r4.ap()
    rkva = rkv.ap()

    ps = [E(nc.psum_tensor(f"ps{i}", [128, 512], F32)) for i in range(8)]

    sems = {}
    for name in ("xld0", "xld1", "wq0", "wq1", "stq0", "stq1",
                 "nat0", "nat1", "fsl0", "fsl1", "fxl0", "fxl1", "fst0", "fst1",
                 "wol0", "wol1", "wxl0", "wxl1", "wst0", "wst1",
                 "kv0", "kv1", "alb0", "alb1",
                 "ldc", "xld", "p1v", "p1t", "p1e",
                 "wq", "mmq", "evq", "stq",
                 "rowld", "p3v", "p3t", "p3e", "p3w", "agkw", "agvw",
                 "ccv", "cck",
                 "natld", "alv", "alw", "abc",
                 "fsl", "fsg", "fmm", "fev", "fxl", "fst",
                 "kvld", "qld", "alb", "smm", "sstt", "sdiag", "sexp",
                 "ptm", "pte", "pvm", "ote",
                 "wol", "wmm", "wev", "wxl", "wst",
                 "bar", "vv", "ss"):
        sems[name] = Sem(E(nc.semaphore(name)))
    sem = sems

    plan = Plan()
    P = plan.emit
    W = plan.wait

    def barrier():
        snap = [(s, s.v) for s in sems.values() if s.v > 0 and s is not sem["bar"]]
        tgt = sem["bar"].v + len(ENGINES)
        sem["bar"].v = tgt
        for e in ENGINES:
            for s, v in snap:
                plan.waits.append((s, v))
                plan.streams[e].append(lambda eng, s=s.h, v=v: eng.wait_ge(s, v))
            plan.streams[e].append(lambda eng, s=sem["bar"].h: eng.nop().then_inc(s, 1))
        for e in ENGINES:
            plan.waits.append((sem["bar"], tgt))
            plan.streams[e].append(lambda eng, s=sem["bar"].h, t=tgt: eng.wait_ge(s, t))

    def checkpoint(k, dumps=()):
        if PH != k:
            return
        barrier()
        dsem = sems["ldc"]
        for dst, srcap in dumps:
            P("sync", lambda e, dst=dst, srcap=srcap, s=dsem.h:
              e.dma_start(out=dst, in_=srcap).then_inc(s, 16))
            dsem.inc(16)
        W("sync", dsem, dsem.v)
        raise _StopBuild

    def vsync():
        v = sem["vv"].inc(1)
        plan.waits.append((sem["vv"], v))
        plan.streams["vector"].append(lambda e, s=sem["vv"].h, v=v: e.wait_ge(s, v))
        return v

    def _body():
        for dst, src in ((ident_f[:, :], ident_in), (maskd[:, :], maskd_in)):
            P("sync", lambda e, dst=dst, src=src, s=sem["ldc"].h: e.dma_start(out=dst, in_=src).then_inc(s, 16))
            sem["ldc"].inc(16)
        for i, src in enumerate((qln_w, qln_b, kln_w, kln_b)):
            ap = src.rearrange("(t p) -> p t", p=128)
            P("sync", lambda e, ap=ap, i=i, s=sem["ldc"].h:
              e.dma_start(out=wb[:, i * 32:(i + 1) * 32], in_=ap, allow_slow_non_contiguous=True).then_inc(s, 16))
            sem["ldc"].inc(16)
        ldc_done = sem["ldc"].v

        W("vector", sem["ldc"], ldc_done)
        P("vector", lambda e: e.tensor_copy(ident_b[:, :], ident_f[:, :]))
        P("vector", lambda e: e.tensor_scalar_mul(wb[:, 0:32], wb[:, 0:32], QSCALE))
        P("vector", lambda e, s=sem["vv"].h: e.tensor_scalar_mul(wb[:, 32:64], wb[:, 32:64], QSCALE).then_inc(s, 1))
        vsync()

        sq_done = []

        def ln_stats(rowap, sqout):
            P("vector", lambda e, rowap=rowap, s=sem["vv"].h: e.reduce_sum(sm[:, 0:1], rowap, AX).then_inc(s, 1))
            vsync()
            P("scalar", lambda e, rowap=rowap, sqout=sqout, s=sem["ss"].h:
              e.activation(out=sqout, in_=rowap, func=ACT.Square, accum_out=sm[:, 1:2]).then_inc(s, 1))
            ssv = sem["ss"].inc(1)
            sq_done.append(ssv)
            P("vector", lambda e, s=sem["vv"].h: e.tensor_scalar_mul(sm[:, 2:3], sm[:, 0:1], 1.0 / D).then_inc(s, 1))
            vsync()
            P("vector", lambda e, s=sem["vv"].h:
              e.tensor_scalar(sm[:, 3:4], sm[:, 2:3], sm[:, 2:3], None, OP.mult).then_inc(s, 1))
            vsync()
            W("vector", sem["ss"], ssv)
            P("vector", lambda e, s=sem["vv"].h:
              e.tensor_scalar(sm[:, 4:5], sm[:, 1:2], 1.0 / D, sm[:, 3:4], OP.mult, OP.subtract).then_inc(s, 1))
            vsync()
            P("vector", lambda e, s=sem["vv"].h: e.tensor_scalar_add(sm[:, 5:6], sm[:, 4:5], LN_EPS).then_inc(s, 1))
            vsync()
            P("vector", lambda e, s=sem["vv"].h: e.reciprocal(sm[:, 6:7], sm[:, 5:6]).then_inc(s, 1))
            vsync()
            W("scalar", sem["vv"], sem["vv"].v)
            P("scalar", lambda e, s=sem["ss"].h: e.sqrt(sm[:, 7:8], sm[:, 6:7]).then_inc(s, 1))
            ssv = sem["ss"].inc(1)
            W("vector", sem["ss"], ssv)

        xrow = [w8f[:, 0:4096], w8f[:, 4096:8192]]
        xhat = [r4a[:, 0:4096], r4a[:, 4096:8192]]
        sqd = w8f[:, 8192:12288]
        r8v = r8.ap().rearrange("p (c n) -> p c n", c=32, n=1024)

        for tc in range(NT):
            pb = tc % 2
            if tc >= 2:
                W("sync", sem["p1v"], tc - 1)
                W("sync", sem["ss"], sq_done[tc - 2])
            xsem = sem["xld0" if tc % 2 == 0 else "xld1"]
            P("sync", lambda e, tc=tc, pb=pb, s=xsem.h:
              e.dma_start(out=xrow[pb], in_=x_in[tc * 128:(tc + 1) * 128, :]).then_inc(s, 16))
            xv = xsem.inc(16)
            W("vector", xsem, xv)
            W("scalar", xsem, xv)
            ln_stats(xrow[pb], sqd)
            if tc >= 2:
                W("vector", sem["p1t"], (tc - 1) * 8)
            P("vector", lambda e, pb=pb, s=sem["p1v"].h:
              e.tensor_scalar(xhat[pb], xrow[pb], sm[:, 2:3], sm[:, 7:8], OP.subtract, OP.mult).then_inc(s, 1))
            p1vv = sem["p1v"].inc(1)
            W("tensor", sem["p1v"], p1vv)
            for grp in range(8):
                bank = grp % 2
                psb = ps[bank].ap().bitcast(BF16)
                if sem["p1t"].v >= 2:
                    W("tensor", sem["p1e"], sem["p1e"].v - 1)
                for q in range(4):
                    dmc = grp * 4 + q
                    if q == 3:
                        P("tensor", lambda e, pb=pb, dmc=dmc, psb=psb, q=q, s=sem["p1t"].h:
                          e.transpose(psb[:, q * 128:(q + 1) * 128],
                                      xhat[pb][:, dmc * 128:(dmc + 1) * 128], ident_b[:, :]).then_inc(s, 1))
                    else:
                        P("tensor", lambda e, pb=pb, dmc=dmc, psb=psb, q=q:
                          e.transpose(psb[:, q * 128:(q + 1) * 128],
                                      xhat[pb][:, dmc * 128:(dmc + 1) * 128], ident_b[:, :]))
                p1tv = sem["p1t"].inc(1)
                W("vector", sem["p1t"], p1tv)
                d0 = grp * 4
                P("vector", lambda e, psb=psb, d0=d0, tc=tc, s=sem["p1e"].h:
                  e.tensor_copy(r8v[:, d0:d0 + 4, tc * 128:tc * 128 + 128],
                                psb[:, 0:512].rearrange("p (c n) -> p c n", c=4, n=128)).then_inc(s, 1))
                sem["p1e"].inc(1)

        checkpoint(1, dumps=[(r8_dump, r8a)] if DUMP else [])

        barrier()

        g_order = list(range(16, 24)) + list(range(0, 16))
        wslab = [w8a[:, 0:16384], w8a[:, 16384:32768]]
        stage_f = [r4f[:, 0:4096], r4f[:, 4096:8192]]
        stage_b = [rkva[:, 0:4096], rkva[:, 4096:8192]]
        wqkvT_r = wqkvT.rearrange("(c p) n -> p c n", p=128)
        agv_r = agv_in.ap().rearrange("(t p) n -> p t n", p=128)
        qsc_r = q_scr.rearrange("(t p) n -> p t n", p=128)
        ksc_r = k_scr.rearrange("(t p) n -> p t n", p=128)

        stg_dma = []

        for gi, g in enumerate(g_order):
            pb = gi % 2
            if gi >= 2:
                W("gpsimd", sem["mmq"], (gi - 1) * 256)
            wqsem = sem["wq0" if gi % 2 == 0 else "wq1"]
            P("gpsimd", lambda e, g=g, pb=pb, s=wqsem.h:
              e.dma_start(out=wslab[pb].rearrange("p (c n) -> p c n", c=32, n=512),
                          in_=wqkvT_r[:, :, g * 512:(g + 1) * 512]).then_inc(s, 16))
            wqv = wqsem.inc(16)
            W("tensor", wqsem, wqv)
            for half in range(2):
                if gi >= 1:
                    W("tensor", sem["evq"], (gi - 1) * 8 + (half + 1) * 4)
                for dmc in range(NDM):
                    for tcq in range(4):
                        tc = half * 4 + tcq
                        P("tensor", lambda e, pb=pb, dmc=dmc, tc=tc, s=sem["mmq"].h:
                          e.matmul(ps[tc][:, :],
                                   r8a[:, dmc * 1024 + tc * 128: dmc * 1024 + tc * 128 + 128],
                                   wslab[pb][:, dmc * 512:(dmc + 1) * 512],
                                   start=(dmc == 0), stop=(dmc == NDM - 1)).then_inc(s, 1))
                        sem["mmq"].inc(1)
                if gi >= 2:
                    W("vector", *stg_dma[gi - 2])
                W("vector", sem["mmq"], sem["mmq"].v)
                for tcq in range(4):
                    tc = half * 4 + tcq
                    dst = (stage_b[pb] if g >= 16 else stage_f[pb])[:, tc * 512:(tc + 1) * 512]
                    P("vector", lambda e, tc=tc, dst=dst, s=sem["evq"].h:
                      e.tensor_scalar(dst, ps[tc][:, :], CLIP, -CLIP, OP.min, OP.max).then_inc(s, 1))
                    sem["evq"].inc(1)
            W("sync", sem["evq"], sem["evq"].v)
            if g >= 16:
                dd = agv_r[:, :, (g - 16) * 512:(g - 15) * 512]
                src = stage_b[pb].rearrange("p (t n) -> p t n", t=8, n=512)
                ssm = sem["stq0" if gi % 2 == 0 else "stq1"]
            else:
                dd = (qsc_r if g < 8 else ksc_r)[:, :, (g % 8) * 512:((g % 8) + 1) * 512]
                src = stage_f[pb].rearrange("p (t n) -> p t n", t=8, n=512)
                ssm = sem["stq0" if gi % 2 == 0 else "stq1"]
            P("sync", lambda e, dd=dd, src=src, s=ssm.h: e.dma_start(out=dd, in_=src).then_inc(s, 16))
            stg_dma.append((ssm, ssm.inc(16)))
            if gi == 7:
                agv_done = [(sem["stq0"], sem["stq0"].v), (sem["stq1"], sem["stq1"].v)]

        checkpoint(2)

        for s_, v_ in agv_done:
            W("gpsimd", s_, v_)
        P("gpsimd", lambda e, s=sem["ccv"].h:
          e.collective_compute("AllGather", OP.bypass,
                               replica_groups=[[0, 1, 2, 3], [4, 5, 6, 7]],
                               ins=[agv_in.ap()], outs=[agv_out.ap()]).then_inc(s, 1))
        sem["ccv"].inc(1)

        barrier()

        rowbuf = [w8f[:, 0:4096], w8f[:, 4096:8192]]
        hatbuf = [w8f[:, 8192:12288], w8f[:, 12288:16384]]
        tstage = [r4a[:, 0:4096], r4a[:, 4096:8192]]
        qT_r = qT_scr.rearrange("(c p) n -> p c n", p=128)
        agk_r = agk_in.ap().rearrange("(c p) n -> p c n", p=128)

        p3row_guard = []
        p3st_dma = {0: [], 1: []}

        for tc in range(NT):
            for qk in range(2):
                idx = tc * 2 + qk
                pb = idx % 2
                if idx >= 2:
                    W("sync", sem["p3v"], idx - 1)
                    W("sync", sem["ss"], sq_done[8 + idx - 2])
                src_scr = q_scr if qk == 0 else k_scr
                P("sync", lambda e, src_scr=src_scr, tc=tc, pb=pb, s=sem["rowld"].h:
                  e.dma_start(out=rowbuf[pb], in_=src_scr[tc * 128:(tc + 1) * 128, :]).then_inc(s, 16))
                rv = sem["rowld"].inc(16)
                W("vector", sem["rowld"], rv)
                W("scalar", sem["rowld"], rv)
                if idx >= 2:
                    W("scalar", sem["p3t"], (idx - 1) * 8)
                ln_stats(rowbuf[pb], hatbuf[pb])
                P("vector", lambda e, pb=pb, s=sem["p3v"].h:
                  e.tensor_scalar(hatbuf[pb], rowbuf[pb], sm[:, 2:3], sm[:, 7:8], OP.subtract, OP.mult).then_inc(s, 1))
                p3vv = sem["p3v"].inc(1)
                W("tensor", sem["p3v"], p3vv)
                wcol = qk * 64
                if tc >= 1:
                    W("vector", *p3st_dma[qk][tc - 1])
                for grp in range(8):
                    bank = grp % 2
                    if sem["p3t"].v >= 2:
                        W("tensor", sem["p3e"], sem["p3e"].v - 1)
                    for q in range(4):
                        dmc = grp * 4 + q
                        if q == 3:
                            P("tensor", lambda e, pb=pb, dmc=dmc, bank=bank, q=q, s=sem["p3t"].h:
                              e.transpose(ps[bank][:, q * 128:(q + 1) * 128],
                                          hatbuf[pb][:, dmc * 128:(dmc + 1) * 128], ident_f[:, :]).then_inc(s, 1))
                        else:
                            P("tensor", lambda e, pb=pb, dmc=dmc, bank=bank, q=q:
                              e.transpose(ps[bank][:, q * 128:(q + 1) * 128],
                                          hatbuf[pb][:, dmc * 128:(dmc + 1) * 128], ident_f[:, :]))
                    p3tv = sem["p3t"].inc(1)
                    W("vector", sem["p3t"], p3tv)
                    for q in range(4):
                        dmc = grp * 4 + q
                        if q == 3:
                            P("vector", lambda e, qk=qk, bank=bank, q=q, dmc=dmc, wcol=wcol, s=sem["p3e"].h:
                              e.tensor_scalar(tstage[qk][:, dmc * 128:(dmc + 1) * 128],
                                              ps[bank][:, q * 128:(q + 1) * 128],
                                              wb[:, wcol + dmc:wcol + dmc + 1],
                                              wb[:, wcol + 32 + dmc:wcol + 32 + dmc + 1],
                                              OP.mult, OP.add).then_inc(s, 1))
                        else:
                            P("vector", lambda e, qk=qk, bank=bank, q=q, dmc=dmc, wcol=wcol:
                              e.tensor_scalar(tstage[qk][:, dmc * 128:(dmc + 1) * 128],
                                              ps[bank][:, q * 128:(q + 1) * 128],
                                              wb[:, wcol + dmc:wcol + dmc + 1],
                                              wb[:, wcol + 32 + dmc:wcol + 32 + dmc + 1],
                                              OP.mult, OP.add))
                    sem["p3e"].inc(1)
                W("sync", sem["p3e"], sem["p3e"].v)
                if qk == 0:
                    dd, ssm = qT_r[:, :, tc * 128:(tc + 1) * 128], sem["p3w"]
                else:
                    dd, ssm = agk_r[:, :, tc * 128:(tc + 1) * 128], sem["agkw"]
                P("sync", lambda e, dd=dd, qk=qk, s=ssm.h:
                  e.dma_start(out=dd, in_=tstage[qk].rearrange("p (c n) -> p c n", c=32, n=128)).then_inc(s, 16))
                p3st_dma[qk].append((ssm, ssm.inc(16)))

        W("gpsimd", sem["agkw"], 8 * 16)
        W("gpsimd", sem["ccv"], 1)
        P("gpsimd", lambda e, s=sem["cck"].h:
          e.collective_compute("AllGather", OP.bypass,
                               replica_groups=[[0, 1, 2, 3], [4, 5, 6, 7]],
                               ins=[agk_in.ap()], outs=[agk_out.ap()]).then_inc(s, 1))
        sem["cck"].inc(1)

        if DUMP and PH == 3:
            W("sync", sem["cck"], 1)
            checkpoint(3, dumps=[(agk_dump, agk_out.ap()), (agv_dump, agv_out.ap())])

        barrier()

        natbuf = [w8f[:, 0:4096], w8f[:, 4096:8192]]
        for t in range(32):
            pb = t % 2
            if t >= 2:
                W("sync", sem["alv"], t - 1)
            nsem = sem["nat0" if t % 2 == 0 else "nat1"]
            P("sync", lambda e, t=t, pb=pb, s=nsem.h:
              e.dma_start(out=natbuf[pb], in_=wffn_nat[t * 128:(t + 1) * 128, :]).then_inc(s, 16))
            nl = nsem.inc(16)
            W("vector", nsem, nl)
            P("vector", lambda e, pb=pb, t=t, s=sem["alv"].h:
              e.reduce_sum(alpha_sb[:, t:t + 1], natbuf[pb], AX, apply_absolute_value=True).then_inc(s, 1))
            sem["alv"].inc(1)
        W("vector", sem["alv"], 32)
        P("vector", lambda e, s=sem["vv"].h:
          e.tensor_scalar_mul(alpha_sb[:, :], alpha_sb[:, :], 1.0 / D).then_inc(s, 1))
        vsync()
        P("vector", lambda e, s=sem["alw"].h: e.nop().then_inc(s, 1))
        sem["alw"].inc(1)
        W("tensor", sem["alw"], sem["alw"].v)
        P("tensor", lambda e, s=sem["alw"].h:
          e.transpose(ps[0][0:32, 0:128], alpha_sb[:, 0:32], ident_f[:, :]).then_inc(s, 1))
        sem["alw"].inc(1)
        W("vector", sem["alw"], sem["alw"].v)
        alpha_row = rkvf[0:32, 2048:2176]
        P("vector", lambda e, s=sem["alw"].h:
          e.tensor_copy(alpha_row, ps[0][0:32, 0:128]).then_inc(s, 1))
        sem["alw"].inc(1)
        W("sync", sem["alw"], sem["alw"].v)
        P("sync", lambda e, s=sem["alw"].h:
          e.dma_start(out=alpha_scr.rearrange("(t q) -> t q", q=128), in_=alpha_row).then_inc(s, 16))
        sem["alw"].inc(16)
        W("sync", sem["alw"], sem["alw"].v)
        alpha_bc = r4f[:, 0:4096]
        bias_bc = r4f[:, 4096:8192]
        P("sync", lambda e, s=sem["abc"].h:
          e.dma_start(out=alpha_bc, in_=alpha_scr.unsqueeze(0).broadcast_to([128, D])).then_inc(s, 16))
        sem["abc"].inc(16)
        P("sync", lambda e, s=sem["abc"].h:
          e.dma_start(out=bias_bc, in_=fbias.unsqueeze(0).broadcast_to([128, D])).then_inc(s, 16))
        sem["abc"].inc(16)

        if PH == 41:
            W("sync", sem["abc"], sem["abc"].v)
            checkpoint(41)

        wffnT_r = wffnT.rearrange("(c p) n -> p c n", p=128)
        slabh = [w8a[:, i * 8192:(i + 1) * 8192] for i in range(2)]
        signh = [w8a[:, 16384 + i * 8192: 16384 + (i + 1) * 8192] for i in range(2)]
        xsl = [rkvf[:, 0:512], rkvf[:, 512:1024]]
        fstage = [rkvf[:, 1024:1536], rkvf[:, 1536:2048]]
        x_r = x_in.rearrange("(t p) n -> p t n", p=128)
        ffnx_r = ffnx_scr.rearrange("(t p) n -> p t n", p=128)

        for g in range(8):
            for hf in range(2):
                n = g * 2 + hf
                pb = n % 2
                if n < 2:
                    W("gpsimd", sem["alv"], 32)
                if n == 0:
                    W("tensor", sem["alw"], 3)
                if n >= 2:
                    W("gpsimd", sem["fmm"], (n - 1) * 128)
                fsem = sem["fsl0" if n % 2 == 0 else "fsl1"]
                P("gpsimd", lambda e, g=g, hf=hf, pb=pb, s=fsem.h:
                  e.dma_start(out=slabh[pb].rearrange("p (c n) -> p c n", c=16, n=512),
                              in_=wffnT_r[:, hf * 16:(hf + 1) * 16, g * 512:(g + 1) * 512]).then_inc(s, 16))
                fv = fsem.inc(16)
                W("scalar", fsem, fv)
                if n >= 2:
                    W("scalar", sem["fmm"], (n - 1) * 128)
                P("scalar", lambda e, pb=pb, s=sem["fsg"].h:
                  e.sign(signh[pb][:, :], slabh[pb][:, :]).then_inc(s, 1))
                sg = sem["fsg"].inc(1)
                W("tensor", sem["fsg"], sg)
                if g >= 1 and hf == 0:
                    W("tensor", sem["fev"], g * 8)
                for dq in range(16):
                    dmc = hf * 16 + dq
                    for tc in range(NT):
                        P("tensor", lambda e, pb=pb, dmc=dmc, dq=dq, tc=tc, s=sem["fmm"].h:
                          e.matmul(ps[tc][:, :],
                                   r8a[:, dmc * 1024 + tc * 128: dmc * 1024 + tc * 128 + 128],
                                   signh[pb][:, dq * 512:(dq + 1) * 512],
                                   start=(dmc == 0), stop=(dmc == NDM - 1)).then_inc(s, 1))
                        sem["fmm"].inc(1)
            W("vector", sem["abc"], 32)
            for tc in range(NT):
                n2 = g * 8 + tc
                pbx = n2 % 2
                if n2 >= 2:
                    W("sync", sem["fev"], n2 - 1)
                fxsem = sem["fxl0" if n2 % 2 == 0 else "fxl1"]
                P("sync", lambda e, g=g, tc=tc, pbx=pbx, s=fxsem.h:
                  e.dma_start(out=xsl[pbx], in_=x_r[:, tc, g * 512:(g + 1) * 512]).then_inc(s, 16))
                xv = fxsem.inc(16)
                W("vector", sem["fmm"], sem["fmm"].v)
                W("vector", fxsem, xv)
                fssem = sem["fst0" if n2 % 2 == 0 else "fst1"]
                if n2 >= 2:
                    W("vector", fssem, fssem.v)
                P("vector", lambda e, g=g, tc=tc, pbx=pbx, s=sem["vv"].h:
                  e.scalar_tensor_tensor(fstage[pbx], ps[tc][:, :], 1.0,
                                         alpha_bc[:, g * 512:(g + 1) * 512], OP.mult, OP.mult).then_inc(s, 1))
                vsync()
                P("vector", lambda e, g=g, pbx=pbx, s=sem["vv"].h:
                  e.tensor_add(fstage[pbx], fstage[pbx], bias_bc[:, g * 512:(g + 1) * 512]).then_inc(s, 1))
                vsync()
                P("vector", lambda e, pbx=pbx, s=sem["fev"].h:
                  e.tensor_add(fstage[pbx], fstage[pbx], xsl[pbx]).then_inc(s, 1))
                sem["fev"].inc(1)
                W("sync", sem["fev"], sem["fev"].v)
                P("sync", lambda e, g=g, tc=tc, pbx=pbx, s=fssem.h:
                  e.dma_start(out=ffnx_r[:, tc, g * 512:(g + 1) * 512], in_=fstage[pbx]).then_inc(s, 16))
                fssem.inc(16)
            if g == 0 and PH == 42:
                W("sync", sem["fst0"], sem["fst0"].v)
                W("sync", sem["fst1"], sem["fst1"].v)
                checkpoint(42)

        checkpoint(4)

        barrier()
        W("tensor", sem["cck"], 1)

        OT = r8a
        Sb = [w8f[:, 0:4096], w8f[:, 4096:8192]]
        alibi_b = [w8f[:, 8192:12288], w8f[:, 12288:16384]]
        Pb = [[r4a[:, 0:4096], r4a[:, 4096:8192]],
              [r4a[:, 8192:12288], r4a[:, 12288:16384]]]
        PT = [r4a[:, 16384:16896], r4a[:, 16896:17408]]
        diag = [[r4a[:, 17408:17536], r4a[:, 17536:17664]],
                [r4a[:, 17664:17792], r4a[:, 17792:17920]]]
        Qh = [r4a[:, 17920:18944], r4a[:, 18944:19968]]
        Kh = [rkva[:, 0:4096], rkva[:, 4096:8192]]
        Vh = [rkva[:, 8192:12288], rkva[:, 12288:16384]]
        combf = comb.ap()
        combA = [combf[:, 0:512], combf[:, 0:512]]
        combB = [combf[:, 512:1024], combf[:, 512:1024]]

        agk_o = agk_out.ap()
        agv_o = agv_out.ap()

        def emit_kv_loads(h):
            pb = h % 2
            ksem = sem["kv0" if h % 2 == 0 else "kv1"]
            for r in range(LANES):
                kdst = Kh[pb].rearrange("p (i n) -> p i n", i=8, n=512)[:, :, r * 128:(r + 1) * 128]
                ksrc = agk_o[r * D + h * 128: r * D + (h + 1) * 128, :].rearrange("p (i n) -> p i n", i=8, n=128)
                P("sync", lambda e, kdst=kdst, ksrc=ksrc, s=ksem.h:
                  e.dma_start(out=kdst, in_=ksrc).then_inc(s, 16))
                ksem.inc(16)
                vdst = Vh[pb].rearrange("p (i n) -> p i n", i=8, n=512)[:, :, r * 128:(r + 1) * 128]
                vsrc = agv_o[r * TLOC:(r + 1) * TLOC, h * 128:(h + 1) * 128].rearrange("(i p) n -> p i n", p=128)
                P("sync", lambda e, vdst=vdst, vsrc=vsrc, s=ksem.h:
                  e.dma_start(out=vdst, in_=vsrc).then_inc(s, 16))
                ksem.inc(16)
            P("sync", lambda e, h=h, pb=pb, s=ksem.h:
              e.dma_start(out=Qh[pb], in_=qT_scr[h * 128:(h + 1) * 128, :]).then_inc(s, 16))
            ksem.inc(16)

        W("sync", sem["cck"], 1)
        emit_kv_loads(0)

        def emit_alibi(h):
            hb = h % 2
            asem = sem["alb0" if h % 2 == 0 else "alb1"]
            P("sync", lambda e, hb=hb, s=asem.h:
              e.dma_start(out=alibi_b[hb], in_=colidx_in).then_inc(s, 16))
            av = asem.inc(16)
            W("vector", asem, av)
            P("vector", lambda e, hb=hb, h=h, s=sem["alb"].h:
              e.tensor_scalar_mul(alibi_b[hb], alibi_b[hb], SLOPES[h]).then_inc(s, 1))
            sem["alb"].inc(1)

        emit_alibi(0)

        def emit_stage_SP(h, j):
            sidx = h * NPAIR + j
            pp = sidx % 2
            hb = h % 2
            nckA, nckB = 2 * j + 1, 2 * j + 2
            kmax = nckB * 512
            W("vector", sem["alb"], sem["alb"].v)
            dA0, dB0 = 2 * j * 512, (2 * j + 1) * 512
            if sidx >= 2:
                pass
            P("vector", lambda e, hb=hb, dA0=dA0, pp=pp, s=sem["vv"].h:
              e.tensor_add(combA[pp], maskd[:, :], alibi_b[hb][:, dA0:dA0 + 512]).then_inc(s, 1))
            vsync()
            P("vector", lambda e, hb=hb, dB0=dB0, pp=pp, s=sem["vv"].h:
              e.tensor_add(combB[pp], maskd[:, :], alibi_b[hb][:, dB0:dB0 + 512]).then_inc(s, 1))
            vsync()
            for t in range(2):
                nck = nckA if t == 0 else nckB
                qcol = (2 * j + t) * 128
                mb0 = 16 + t * 8
                W("vector", sem["sexp"], max(0, 2 * (sidx - 1) + t + 1))
                for kc in range(nck):
                    bank = sem["smm"].v % 2
                    if sem["smm"].v >= 2:
                        W("tensor", sem["sstt"], sem["smm"].v - 1)
                    P("tensor", lambda e, hb=hb, qcol=qcol, kc=kc, bank=bank, s=sem["smm"].h:
                      e.matmul(ps[bank][:, :], Qh[hb][:, qcol:qcol + 128],
                               Kh[hb][:, kc * 512:(kc + 1) * 512], start=True, stop=True).then_inc(s, 1))
                    mmv = sem["smm"].inc(1)
                    W("vector", sem["smm"], mmv)
                    isdiag = (kc == nck - 1)
                    in1 = (combA[pp] if t == 0 else combB[pp]) if isdiag else None
                    if isdiag:
                        P("vector", lambda e, t=t, bank=bank, kc=kc, in1=in1, s=sem["sstt"].h:
                          e.scalar_tensor_tensor(Sb[t][:, kc * 512:(kc + 1) * 512], ps[bank][:, :], 1.0,
                                                 in1, OP.mult, OP.add).then_inc(s, 1))
                    else:
                        P("vector", lambda e, t=t, bank=bank, kc=kc, hb=hb, s=sem["sstt"].h:
                          e.scalar_tensor_tensor(Sb[t][:, kc * 512:(kc + 1) * 512], ps[bank][:, :], 1.0,
                                                 alibi_b[hb][:, kc * 512:(kc + 1) * 512],
                                                 OP.mult, OP.add).then_inc(s, 1))
                    sem["sstt"].inc(1)
                W("vector", sem["sstt"], sem["sstt"].v)
                P("vector", lambda e, t=t, nck=nck, s=sem["vv"].h:
                  e.reduce_max(sm[:, 32 + t:33 + t], Sb[t][:, 0:nck * 512], AX, negate=True).then_inc(s, 1))
                nmv = vsync()
                W("scalar", sem["vv"], nmv)
                W("scalar", sem["ptm"], sem["ptm"].v)
                P("scalar", lambda e, t=t, pp=pp, nck=nck, s=sem["sexp"].h:
                  e.activation(out=Pb[pp][t][:, 0:nck * 512], in_=Sb[t][:, 0:nck * 512], func=ACT.Exp,
                               bias=sm[:, 32 + t:33 + t], scale=1.0,
                               accum_out=sm[:, 34 + t:35 + t]).then_inc(s, 1))
                exv = sem["sexp"].inc(1)
                if t == 0:
                    W("vector", sem["ptm"], sem["ptm"].v)
                    P("vector", lambda e, pp=pp, nckA=nckA, kmax=kmax:
                      e.memset(Pb[pp][0][:, nckA * 512:kmax], 0.0))
                W("vector", sem["sexp"], exv)
                P("vector", lambda e, t=t, s=sem["vv"].h:
                  e.reciprocal(sm[:, 36 + t:37 + t], sm[:, 34 + t:35 + t]).then_inc(s, 1))
                vsync()
                P("vector", lambda e, t=t, pp=pp, s=sem["sdiag"].h:
                  e.tensor_scalar_mul(diag[pp][t], ident_b[:, :], sm[:, 36 + t:37 + t]).then_inc(s, 1))
                sem["sdiag"].inc(1)
            return dict(h=h, j=j, pp=pp, hb=hb, sidx=sidx, sdiag=sem["sdiag"].v)

        def emit_stage_PT_PV(info):
            h, j, pp, hb, sidx = info["h"], info["j"], info["pp"], info["hb"], info["sidx"]
            nkc = (2 * j + 2) * 4
            otb = 4 + sidx % 2
            W("tensor", sem["sdiag"], info["sdiag"])
            if sidx >= 2:
                W("tensor", sem["ote"], sidx - 1)
            for kc2 in range(0, nkc, 2):
                ptb = 2 + (kc2 // 2) % 2
                ptp = (kc2 // 2) % 2
                if sem["pte"].v >= 2:
                    W("tensor", sem["pte"], sem["pte"].v - 1)
                for sub in range(2):
                    kc = kc2 + sub
                    for t in range(2):
                        if sub == 1 and t == 1:
                            P("tensor", lambda e, pp=pp, t=t, kc=kc, sub=sub, ptb=ptb, s=sem["ptm"].h:
                              e.matmul(ps[ptb][:, sub * 256 + t * 128: sub * 256 + (t + 1) * 128],
                                       Pb[pp][t][:, kc * 128:(kc + 1) * 128], diag[pp][t],
                                       start=True, stop=True).then_inc(s, 1))
                        else:
                            P("tensor", lambda e, pp=pp, t=t, kc=kc, sub=sub, ptb=ptb:
                              e.matmul(ps[ptb][:, sub * 256 + t * 128: sub * 256 + (t + 1) * 128],
                                       Pb[pp][t][:, kc * 128:(kc + 1) * 128], diag[pp][t],
                                       start=True, stop=True))
                ptv = sem["ptm"].inc(1)
                W("scalar", sem["ptm"], ptv)
                if sem["pte"].v >= 2:
                    W("scalar", sem["pvm"], sem["pvm"].v - 1)
                P("scalar", lambda e, ptb=ptb, ptp=ptp, s=sem["pte"].h:
                  e.copy(PT[ptp][:, :], ps[ptb][:, :]).then_inc(s, 1))
                ptev = sem["pte"].inc(1)
                W("tensor", sem["pte"], ptev)
                for sub in range(2):
                    kc = kc2 + sub
                    if sub == 1:
                        P("tensor", lambda e, hb=hb, kc=kc, sub=sub, ptp=ptp, otb=otb,
                          st=(kc == 0), sp=(kc == nkc - 1), s=sem["pvm"].h:
                          e.matmul(ps[otb][:, 0:256], Vh[hb][:, kc * 128:(kc + 1) * 128],
                                   PT[ptp][:, sub * 256:(sub + 1) * 256], start=st, stop=sp).then_inc(s, 1))
                    else:
                        P("tensor", lambda e, hb=hb, kc=kc, sub=sub, ptp=ptp, otb=otb,
                          st=(kc == 0), sp=(kc == nkc - 1):
                          e.matmul(ps[otb][:, 0:256], Vh[hb][:, kc * 128:(kc + 1) * 128],
                                   PT[ptp][:, sub * 256:(sub + 1) * 256], start=st, stop=sp))
                sem["pvm"].inc(1)
            W("vector", sem["pvm"], sem["pvm"].v)
            P("vector", lambda e, h=h, j=j, otb=otb, s=sem["ote"].h:
              e.tensor_copy(OT[:, h * 1024 + j * 256: h * 1024 + (j + 1) * 256],
                            ps[otb][:, 0:256]).then_inc(s, 1))
            sem["ote"].inc(1)

        H_eff = {51: 1, 52: 4, 53: 16}.get(PH, H)
        prev_info = None
        for h in range(H_eff):
            W("tensor", sem["kv0" if h % 2 == 0 else "kv1"], 144 * (h // 2 + 1))
            for j in range(NPAIR):
                info = emit_stage_SP(h, j)
                if prev_info is not None:
                    emit_stage_PT_PV(prev_info)
                prev_info = info
                if j == 0 and h + 1 < H_eff:
                    if h >= 1:
                        W("sync", sem["ote"], h * NPAIR)
                    emit_kv_loads(h + 1)
                    emit_alibi(h + 1)
        emit_stage_PT_PV(prev_info)

        if PH in (51, 52, 53):
            checkpoint(PH, dumps=[(r8_dump, r8a), (sm_dump, sm[:, :]),
                                  (r4_dump, r4a), (w8_dump, w8a)] if DUMP else [])
        checkpoint(5, dumps=[(r8_dump, r8a)] if DUMP else [])

        barrier()

        woutT_r = woutT.rearrange("(c p) n -> p c n", p=128)
        wslab2 = [w8a[:, 0:16384], w8a[:, 16384:32768]]
        fxsl = [rkvf[:, 0:512], rkvf[:, 512:1024]]
        ostage = [rkvf[:, 1024:1536], rkvf[:, 1536:2048]]
        out_r = out_d.rearrange("(t p) n -> p t n", p=128)

        for g in range(8):
            pb = g % 2
            if g >= 2:
                W("gpsimd", sem["wmm"], (g - 1) * 256)
            wlsem = sem["wol0" if g % 2 == 0 else "wol1"]
            P("gpsimd", lambda e, g=g, pb=pb, s=wlsem.h:
              e.dma_start(out=wslab2[pb].rearrange("p (c n) -> p c n", c=32, n=512),
                          in_=woutT_r[:, :, g * 512:(g + 1) * 512]).then_inc(s, 16))
            wv = wlsem.inc(16)
            W("tensor", wlsem, wv)
            if g >= 1:
                W("tensor", sem["wev"], g * 8)
            for dmc in range(NDM):
                for tc in range(NT):
                    P("tensor", lambda e, pb=pb, dmc=dmc, tc=tc, s=sem["wmm"].h:
                      e.matmul(ps[tc][:, :],
                               r8a[:, dmc * 1024 + tc * 128: dmc * 1024 + tc * 128 + 128],
                               wslab2[pb][:, dmc * 512:(dmc + 1) * 512],
                               start=(dmc == 0), stop=(dmc == NDM - 1)).then_inc(s, 1))
                    sem["wmm"].inc(1)
            for tc in range(NT):
                n2 = g * 8 + tc
                pbx = n2 % 2
                if n2 >= 2:
                    W("sync", sem["wev"], n2 - 1)
                wxsem = sem["wxl0" if n2 % 2 == 0 else "wxl1"]
                P("sync", lambda e, g=g, tc=tc, pbx=pbx, s=wxsem.h:
                  e.dma_start(out=fxsl[pbx], in_=ffnx_r[:, tc, g * 512:(g + 1) * 512]).then_inc(s, 16))
                xv = wxsem.inc(16)
                W("vector", sem["wmm"], sem["wmm"].v)
                W("vector", wxsem, xv)
                wssem = sem["wst0" if n2 % 2 == 0 else "wst1"]
                if n2 >= 2:
                    W("vector", wssem, wssem.v)
                P("vector", lambda e, tc=tc, pbx=pbx, s=sem["wev"].h:
                  e.tensor_add(ostage[pbx], ps[tc][:, :], fxsl[pbx]).then_inc(s, 1))
                sem["wev"].inc(1)
                W("sync", sem["wev"], sem["wev"].v)
                P("sync", lambda e, g=g, tc=tc, pbx=pbx, s=wssem.h:
                  e.dma_start(out=out_r[:, tc, g * 512:(g + 1) * 512], in_=ostage[pbx]).then_inc(s, 16))
                wssem.inc(16)

        W("sync", sem["wst0"], sem["wst0"].v)
        W("sync", sem["wst1"], sem["wst1"].v)


    try:
        _body()
    except _StopBuild:
        pass

    for s, v in plan.waits:
        assert v <= s.v, f"wait {v} > final {s.v} on sem"

    block = E(nc.Block())

    @block.tensor
    def _(t):
        for fn in plan.streams["tensor"]:
            fn(t)

    @block.vector
    def _(v):
        for fn in plan.streams["vector"]:
            fn(v)

    @block.scalar
    def _(s):
        for fn in plan.streams["scalar"]:
            fn(s)

    @block.sync
    def _(s):
        for fn in plan.streams["sync"]:
            fn(s)

    @block.gpsimd
    def _(g):
        for fn in plan.streams["gpsimd"]:
            fn(g)

    ctx.close()
    return nc


_NC_CACHE = None


def kernel(L0_attn_hidden_states, L0_attn_Wqkv, L0_attn_q_ln_w, L0_attn_q_ln_b,
           L0_attn_k_ln_w, L0_attn_k_ln_b, L0_attn_Wout, L0_ffn_weight_real,
           L0_ffn_bias):
    global _NC_CACHE
    x = np.ascontiguousarray(L0_attn_hidden_states, dtype=np.float32)
    wqkvT = np.ascontiguousarray(np.asarray(L0_attn_Wqkv, dtype=np.float32).T)
    woutT = np.ascontiguousarray(np.asarray(L0_attn_Wout, dtype=np.float32).T)
    wffn = np.ascontiguousarray(np.asarray(L0_ffn_weight_real, dtype=np.float32))
    wffnT = np.ascontiguousarray(wffn.T)
    qlw = np.ascontiguousarray(L0_attn_q_ln_w, dtype=np.float32)
    qlb = np.ascontiguousarray(L0_attn_q_ln_b, dtype=np.float32)
    klw = np.ascontiguousarray(L0_attn_k_ln_w, dtype=np.float32)
    klb = np.ascontiguousarray(L0_attn_k_ln_b, dtype=np.float32)
    fb = np.ascontiguousarray(L0_ffn_bias, dtype=np.float32)
    ident = np.eye(128, dtype=np.float32)
    colidx = np.broadcast_to(np.arange(D, dtype=np.float32), (128, D)).copy()

    in_maps = []
    for c in range(NCORE):
        b, lane = c // 4, c % 4
        xs = np.concatenate([x[b, (lane + 4 * i) * 128:(lane + 4 * i) * 128 + 128, :]
                             for i in range(NT)], axis=0)
        md = np.where(np.arange(512)[None, :] <= lane * 128 + np.arange(128)[:, None],
                      0.0, NEG_BIG).astype(np.float32)
        in_maps.append({
            "x_shard": np.ascontiguousarray(xs),
            "wqkvT": wqkvT, "woutT": woutT, "wffnT": wffnT, "wffn_nat": wffn,
            "qln_w": qlw, "qln_b": qlb, "kln_w": klw, "kln_b": klb, "fbias": fb,
            "ident": ident, "colidx": colidx, "maskd": md,
        })

    if _NC_CACHE is None:
        _NC_CACHE = build_program()
    res = run_bass_kernel_spmd(_NC_CACHE, in_maps, core_ids=list(range(NCORE)))

    out = np.empty((B, S, D), dtype=np.float32)
    for c in range(NCORE):
        b, lane = c // 4, c % 4
        sh = res.results[c]["out"]
        for i in range(NT):
            out[b, (lane + 4 * i) * 128:(lane + 4 * i) * 128 + 128, :] = sh[i * 128:(i + 1) * 128, :]
    return out
